# version 05
# baseline (speedup 1.0000x reference)
import numpy as np
from contextlib import ExitStack

import concourse.bass as bass
import concourse.mybir as mybir
from concourse.bass_utils import run_bass_kernel_spmd

F32 = mybir.dt.float32
BF16 = mybir.dt.bfloat16
I32 = mybir.dt.int32
AF = mybir.ActivationFunctionType
ALU = mybir.AluOpType
AX = mybir.AxisListType

NCORE = 8
S = 8192
PC = 1024
D = 2048
H = 16
KVH = 4
FT = D // 128
HT = 8192 // 128
NT = S // PC
TT = PC // 128
NCG = 2
Q127 = 127.0
QS = 1.0 / (127.0 * 127.0)
X2_RES = 44
EPS_LN = 1e-5
EPS_AT = 1e-6


def build_nc():
    nc = bass.Bass("TRN2", target_bir_lowering=False)

    xT = nc.dram_tensor("xT", [D, S], F32, kind="ExternalInput").ap()
    wq = nc.dram_tensor("wq", [H, 128, D], F32, kind="ExternalInput").ap()
    wo = nc.dram_tensor("wo", [FT, 128, D], F32, kind="ExternalInput").ap()
    w1 = nc.dram_tensor("w1", [HT, 128, D], F32, kind="ExternalInput").ap()
    w2 = nc.dram_tensor("w2", [4, 128, HT, 512], F32, kind="ExternalInput").ap()
    wk = nc.dram_tensor("wk", [FT, 128, 512], F32, kind="ExternalInput").ap()
    wv = nc.dram_tensor("wv", [FT, 128, 512], F32, kind="ExternalInput").ap()
    bqh = nc.dram_tensor("bqh", [128, H], F32, kind="ExternalInput").ap()
    bkv = nc.dram_tensor("bkv", [1, 1024], F32, kind="ExternalInput").ap()
    b1h = nc.dram_tensor("b1h", [128, HT], F32, kind="ExternalInput").ap()
    b2h = nc.dram_tensor("b2h", [128, FT], F32, kind="ExternalInput").ap()
    idn = nc.dram_tensor("idn", [128, 128], F32, kind="ExternalInput").ap()
    y = nc.dram_tensor("y", [D, PC], F32, kind="ExternalOutput").ap()

    d_bc = nc.dram_tensor("d_bc", [8, 2048], F32, kind="Internal").ap()
    d_bcb = nc.dram_tensor("d_bcb", [8, 2048], BF16, kind="Internal").ap()
    d_at = nc.dram_tensor("d_at", [FT, 128, PC], BF16, kind="Internal").ap()
    d_x2 = nc.dram_tensor("d_x2", [HT - X2_RES, 128, PC], BF16, kind="Internal").ap()

    es = ExitStack()
    sbuf = lambda n, sh, dt: es.enter_context(nc.sbuf_tensor(n, sh, dt))
    PB = [sbuf(f"pb{i}", [128, PC], BF16) for i in range(62)]
    ftmp = [sbuf(f"ft{i}", [128, PC], F32) for i in range(2)]
    xf32 = sbuf("xf32", [128, PC], F32)
    r_bc = sbuf("r_bc", [128, PC], F32)
    pww = sbuf("pww", [128, D], F32)
    pwb = [sbuf(f"pwb{i}", [128, D], BF16) for i in range(2)]
    xi32 = sbuf("xi32", [128, PC], I32)
    rows = sbuf("rows", [1, 4096], F32)
    rowsb = sbuf("rowsb", [1, 2048], BF16)
    kvb = sbuf("kvb", [128, 4 * 132], BF16)
    vext = sbuf("vext", [128, 4 * 132], BF16)
    skb = sbuf("skb", [128, 512], BF16)
    amc = sbuf("amc", [128, 16], F32)
    tr8 = sbuf("tr8", [8, 128], F32)
    bqs = sbuf("bqs", [128, H], F32)
    b1s = sbuf("b1s", [128, HT], F32)
    b2s = sbuf("b2s", [128, FT], F32)
    ids = sbuf("ids", [128, 128], F32)
    onesb = sbuf("onesb", [128, 1], BF16)
    bkvb = sbuf("bkvb", [128, 1024], F32)
    awsb = sbuf("awsb", [1, 8], F32)
    c1col = sbuf("c1col", [128, 2], F32)
    smb = sbuf("smb", [128, PC], BF16)
    mbc = sbuf("mbc", [128, PC], BF16)
    rsbc = sbuf("rsbc", [128, PC], BF16)

    pp = [es.enter_context(nc.psum_tensor(f"pp{i}", [128, 512], F32)) for i in range(8)]

    sems = {k: es.enter_context(nc.semaphore(f"s{k}")) for k in "LGVSTP"}
    C = {k: 0 for k in "LGVSTP"}
    cmds = {"sy": [], "ve": [], "se": [], "gq": [], "te": []}
    QSEM = {"sy": "L", "ve": "V", "se": "S", "gq": "P", "te": "T"}

    def op(q, fn, waits=()):
        sk = QSEM[q]
        w = list(waits) + [(sk, C[sk])]
        C[sk] += 1
        w = tuple((k, v) for k, v in w if v)

        def run(e, fn=fn, w=w, sem=sems[sk]):
            for k, v in w:
                e.wait_ge(sems[k], v)
            fn(e).then_inc(sem, 1)
        cmds[q].append(run)
        return C[sk]

    def dma(q, out, in_, waits=(), chain=True):
        sk = "L" if q == "sy" else "G"
        w = list(waits)
        if chain:
            w.append((sk, C[sk]))
        C[sk] += 16
        w = tuple((k, v) for k, v in w if v)

        def run(e, out=out, in_=in_, w=w, sem=sems[sk]):
            for k, v in w:
                e.wait_ge(sems[k], v)
            e.dma_start(out, in_).then_inc(sem, 16)
        cmds[q].append(run)
        return C[sk]

    def bar():
        snap = dict(C)
        for q in cmds:
            def run(e, snap=snap):
                for k, v in snap.items():
                    if v:
                        e.wait_ge(sems[k], v)
            cmds[q].append(run)

    bcslot = [0]
    bc_rd = {}

    def rt_bcast(row_ap, out_ap, n, bf=False, waits=()):
        sl = bcslot[0] % 8
        bcslot[0] += 1
        dd = d_bcb if bf else d_bc
        w = list(waits) + ([("G", bc_rd[sl])] if sl in bc_rd else [])
        t1 = dma("gq", dd[sl:sl + 1, 0:n], row_ap, waits=w)
        t2 = dma("gq", out_ap, dd[sl:sl + 1, 0:n].broadcast_to([128, n]), waits=[("G", t1)])
        bc_rd[sl] = t2
        return t2

    dma("sy", bqs[:, :], bqh[:, :])
    dma("sy", b1s[:, :], b1h[:, :])
    dma("sy", b2s[:, :], b2h[:, :])
    dma("sy", ids[:, :], idn[:, :])
    op("ve", lambda e: e.memset(onesb[:, :], 1.0))
    for j in range(KVH):
        op("ve", lambda e, j=j: e.memset(vext[:, j * 132 + 128:j * 132 + 129], 1.0))
    op("ve", lambda e: e.memset(amc[:, 9:10], 0.0))
    op("ve", lambda e: e.memset(amc[:, 10:11], 0.0))
    dma("gq", bkvb[:, :], bkv[0:1, :].broadcast_to([128, 1024]))

    WKB = PB[46:54]
    WVB = PB[54:62]
    last_cast = {}
    for u in range(32):
        src = wk if u < 16 else wv
        ki = u % 16
        slot = u % 4
        w = [("S", last_cast.get(slot, 0))]
        tl = dma("sy", pww[:, slot * 512:(slot + 1) * 512], src[ki], waits=w)
        dstp = (WKB if u < 16 else WVB)[ki // 2]
        half = (ki % 2) * 512
        last_cast[slot] = op(
            "se", lambda e, s=slot, dp=dstp, h=half:
            e.copy(dp[:, h:h + 512], pww[:, s * 512:(s + 1) * 512]),
            waits=[("L", tl)])
    bar()

    XB = PB[0:16]
    XNO = PB[16:32]
    SQR = PB[32:34]
    wchunk = [0]
    last_wred = [0]

    def emit_wchunk():
        i = wchunk[0]
        if i >= 128:
            return
        wchunk[0] += 1
        if i < 64:
            tl = dma("sy", pww[:, :], w1[i], waits=[("V", last_wred[0])])
        else:
            j = i - 64
            b, kg = j // 16, j % 16
            for kl in range(4):
                tl = dma("sy", pww[:, kl * 512:(kl + 1) * 512], w2[b, :, kg * 4 + kl, :],
                         waits=[("V", last_wred[0])] if kl == 0 else (), chain=(kl == 0))
        t_red = op("ve", lambda e: e.tensor_reduce(amc[:, 8:9], pww[:, :], AX.X, ALU.max,
                                                   apply_absolute_value=True), waits=[("L", tl)])
        dst = 9 if i < 64 else 10
        last_wred[0] = op("ve", lambda e, d=dst: e.tensor_tensor(
            amc[:, d:d + 1], amc[:, d:d + 1], amc[:, 8:9], ALU.max), waits=[("V", t_red)])

    t_amacc = 0
    t_ordr = list(range(1, NT)) + [0]
    for idx, T in enumerate(t_ordr):
        first, last = idx == 0, idx == NT - 1
        xsrc = XNO if T == 0 else XB
        t_sq = {}
        t_cast = {}
        t_smm = {}
        for ki in range(FT):
            w = []
            if ki > 0:
                w = [("S", t_sq[ki - 1]), ("V", t_cast[ki - 1])]
            tld = dma("sy", xf32[:, :], xT[128 * ki:128 * (ki + 1), PC * T:PC * (T + 1)], waits=w)
            t_sq[ki] = op("se", lambda e, ki=ki: e.activation(SQR[ki % 2][:, :], xf32[:, :],
                                                              AF.Square),
                          waits=[("L", tld), ("T", t_smm.get(ki - 2, 0))])
            t_cast[ki] = op("ve", lambda e, ki=ki: e.tensor_copy(XB[ki][:, :], xf32[:, :]),
                            waits=[("L", tld)])
            op("te", lambda e, ki=ki: e.matmul(pp[0][0:1, :], onesb[:, :], XB[ki][:, 0:512],
                                               start=(ki == 0), stop=(ki == FT - 1),
                                               skip_group_check=True),
               waits=[("V", t_cast[ki])])
            op("te", lambda e, ki=ki: e.matmul(pp[2][0:1, :], onesb[:, :], XB[ki][:, 512:1024],
                                               start=(ki == 0), stop=(ki == FT - 1),
                                               skip_group_check=True))
            op("te", lambda e, ki=ki: e.matmul(pp[1][0:1, :], onesb[:, :], SQR[ki % 2][:, 0:512],
                                               start=(ki == 0), stop=(ki == FT - 1),
                                               skip_group_check=True),
               waits=[("S", t_sq[ki])])
            t_smm[ki] = op("te", lambda e, ki=ki: e.matmul(pp[3][0:1, :], onesb[:, :],
                                                           SQR[ki % 2][:, 512:1024],
                                                           start=(ki == 0), stop=(ki == FT - 1),
                                                           skip_group_check=True))
        for _ in range(16):
            emit_wchunk()
        t_mm = C["T"]
        t_stat = 0
        for cg in range(NCG):
            h0 = cg * 512
            ps, pq = (0, 1) if cg == 0 else (2, 3)
            op("ve", lambda e, h0=h0, ps=ps: e.tensor_scalar(rows[0:1, h0:h0 + 512],
                                                             pp[ps][0:1, :], 1.0 / D, None,
                                                             ALU.mult),
               waits=[("T", t_mm)])
            op("ve", lambda e, h0=h0, pq=pq: e.tensor_scalar(rows[0:1, 1024 + h0:1024 + h0 + 512],
                                                             pp[pq][0:1, :], 1.0 / D, None,
                                                             ALU.mult))
            op("ve", lambda e, h0=h0: e.tensor_tensor(rows[0:1, 2048 + h0:2048 + h0 + 512],
                                                      rows[0:1, h0:h0 + 512],
                                                      rows[0:1, h0:h0 + 512], ALU.mult))
            t_stat = op("ve", lambda e, h0=h0: e.tensor_tensor(
                rows[0:1, 2048 + h0:2048 + h0 + 512], rows[0:1, 1024 + h0:1024 + h0 + 512],
                rows[0:1, 2048 + h0:2048 + h0 + 512], ALU.subtract))
        t_ve = op("ve", lambda e: e.tensor_scalar(rows[0:1, 2048:3072], rows[0:1, 2048:3072],
                                                  EPS_LN, None, ALU.add))
        t_sq2 = op("se", lambda e: e.activation(rows[0:1, 2048:3072], rows[0:1, 2048:3072],
                                                AF.Sqrt), waits=[("V", t_ve)])
        t_rstd = op("ve", lambda e: e.reciprocal(rows[0:1, 2048:3072], rows[0:1, 2048:3072]),
                    waits=[("S", t_sq2)])
        t_mrow = op("se", lambda e: e.copy(rowsb[0:1, 0:1024], rows[0:1, 0:1024]))
        t_rrow = op("se", lambda e: e.copy(rowsb[0:1, 1024:2048], rows[0:1, 2048:3072]),
                    waits=[("V", t_rstd)])
        rt_bcast(rowsb[0:1, 0:1024], mbc[:, :], 1024, bf=True, waits=[("S", t_mrow)])
        t_rb = rt_bcast(rowsb[0:1, 1024:2048], rsbc[:, :], 1024, bf=True, waits=[("S", t_rrow)])
        if T == 0:
            op("ve", lambda e: e.memset(r_bc[:, :], 0.0))
        t_xn = {}
        t_prev_g = 0
        t_prev_s = 0
        for ki in range(FT):
            if T == 0:
                trl = dma("sy", xf32[:, :], xT[128 * ki:128 * (ki + 1), 0:PC],
                          waits=[("S", t_prev_s), ("V", t_amacc),
                                 ("S", t_sq[FT - 1]), ("V", t_cast[FT - 1])])
                t_c = op("ve", lambda e: e.tensor_tensor(ftmp[0][:, :], xf32[:, :], mbc[:, :],
                                                         ALU.subtract),
                         waits=[("G", t_rb), ("L", trl)])
                t_c2 = op("ve", lambda e: e.tensor_tensor(ftmp[1][:, :], ftmp[0][:, :],
                                                          rsbc[:, :], ALU.mult))
                t_prev_s = op("se", lambda e, ki=ki: e.copy(XNO[ki][:, :], ftmp[1][:, :]),
                              waits=[("V", t_c2)])
                t_ab = op("se", lambda e: e.activation(ftmp[0][:, :], ftmp[1][:, :], AF.Abs))
                t_amacc = op("ve", lambda e: e.tensor_tensor(
                    r_bc[:, :], r_bc[:, :], ftmp[0][:, :], ALU.max), waits=[("S", t_ab)])
                t_xn[ki] = t_prev_s
            else:
                t_c = op("ve", lambda e, ki=ki: e.tensor_tensor(ftmp[0][:, :], XB[ki][:, :],
                                                                mbc[:, :], ALU.subtract),
                         waits=[("G", t_rb), ("P", t_prev_g)])
                t_prev_g = op("gq", lambda e, dst=xsrc[ki]: e.tensor_tensor(
                    dst[:, :], ftmp[0][:, :], rsbc[:, :], ALU.mult), waits=[("V", t_c)])
                t_xn[ki] = t_prev_g
        t_kv_rd = t_stat
        t_sk_prev = 0
        t_kvmm = 0
        xnsem = "S" if T == 0 else "P"
        for tt in range(TT):
            for ki in range(FT):
                op("te", lambda e, ki=ki, tt=tt, xs=xsrc: e.matmul(
                    pp[2][:, :], xs[ki][:, 128 * tt:128 * (tt + 1)],
                    WKB[ki // 2][:, (ki % 2) * 512:(ki % 2) * 512 + 512],
                    start=(ki == 0), stop=(ki == FT - 1), skip_group_check=True),
                   waits=[(xnsem, t_xn[ki])] + ([("V", t_kv_rd), ("P", t_sk_prev)]
                                               if ki == 0 else []))
                op("te", lambda e, ki=ki, tt=tt, xs=xsrc: e.matmul(
                    pp[3][:, :], xs[ki][:, 128 * tt:128 * (tt + 1)],
                    WVB[ki // 2][:, (ki % 2) * 512:(ki % 2) * 512 + 512],
                    start=(ki == 0), stop=(ki == FT - 1), skip_group_check=True))
            t_kp = C["T"]
            t_a = op("ve", lambda e: e.tensor_tensor(ftmp[0][:, 0:512], pp[2][:, :], bkvb[:, 0:512],
                                                     ALU.add),
                     waits=[("T", t_kp), ("P", t_sk_prev)])
            t_b = op("ve", lambda e: e.tensor_scalar(ftmp[1][:, 0:512], ftmp[0][:, 0:512], 0.0,
                                                     None, ALU.min))
            t_e = op("se", lambda e: e.activation(ftmp[1][:, 0:512], ftmp[1][:, 0:512], AF.Exp),
                     waits=[("V", t_b)])
            t_m = op("gq", lambda e: e.tensor_scalar(ftmp[0][:, 0:512], ftmp[0][:, 0:512], 0.0,
                                                     None, ALU.max),
                     waits=[("V", t_b), ("T", t_kvmm)])
            t_sk = op("gq", lambda e: e.tensor_tensor(skb[:, :], ftmp[0][:, 0:512],
                                                      ftmp[1][:, 0:512], ALU.add),
                      waits=[("S", t_e)])
            t_vx = 0
            for j in range(KVH):
                t_vx = op("ve", lambda e, j=j: e.tensor_tensor(
                    vext[:, 132 * j:132 * j + 128], pp[3][:, 128 * j:128 * (j + 1)],
                    bkvb[:, 512 + 128 * j:512 + 128 * (j + 1)], ALU.add),
                    waits=[("T", t_kp)] + ([("T", t_kvmm)] if j == 0 else []))
            for j in range(KVH):
                op("te", lambda e, j=j, st=(first and tt == 0), sp=(last and tt == TT - 1):
                   e.matmul(
                    pp[4 + j][:, 0:129], skb[:, 128 * j:128 * (j + 1)],
                    vext[:, 132 * j:132 * j + 129],
                    start=st, stop=sp, skip_group_check=True),
                   waits=[("P", t_sk), ("V", t_vx)] if j == 0 else ())
            t_kvmm = C["T"]
            t_kv_rd = max(t_a, t_vx)
            t_sk_prev = t_sk
        bar()
    while wchunk[0] < 128:
        emit_wchunk()
    bar()

    for j in range(KVH):
        op("ve", lambda e, j=j: e.tensor_copy(kvb[:, 132 * j:132 * j + 129], pp[4 + j][:, 0:129]))
    t_red = C["V"]
    for u in range(8):
        t_t = op("te", lambda e, u=u: e.transpose(pp[0][:, 0:128], r_bc[:, 128 * u:128 * (u + 1)],
                                                  ids[:, :]),
                 waits=[("V", t_red)])
        t_red = op("ve", lambda e, u=u: e.tensor_reduce(amc[:, u:u + 1], pp[0][:, 0:128], AX.X,
                                                        ALU.max, apply_absolute_value=True),
                   waits=[("T", t_t)])
    t_am1_tp = C["T"]
    t_v = C["V"]
    t_t = op("te", lambda e: e.transpose(pp[1][0:8, 0:128], amc[:, 0:8], ids[:, :]),
             waits=[("V", t_v)])
    t_v = op("ve", lambda e: e.tensor_copy(tr8[:, :], pp[1][0:8, 0:128]), waits=[("T", t_t)])
    t_g = 0
    for u in range(8):
        t_g = dma("gq", rows[0:1, 128 * u:128 * (u + 1)], tr8[u:u + 1, :], waits=[("V", t_v)])
    t_t = op("te", lambda e: e.transpose(pp[2][0:1, 0:128], amc[:, 9:10], ids[:, :]))
    t_t2 = op("te", lambda e: e.transpose(pp[3][0:1, 0:128], amc[:, 10:11], ids[:, :]))
    op("ve", lambda e: e.tensor_reduce(awsb[0:1, 0:1], pp[2][0:1, 0:128], AX.X, ALU.max,
                                       apply_absolute_value=True), waits=[("T", t_t2)])
    op("ve", lambda e: e.tensor_reduce(awsb[0:1, 1:2], pp[3][0:1, 0:128], AX.X, ALU.max,
                                       apply_absolute_value=True))
    op("ve", lambda e: e.reciprocal(awsb[0:1, 2:4], awsb[0:1, 0:2]))
    t_c12 = op("ve", lambda e: e.tensor_scalar(awsb[0:1, 4:6], awsb[0:1, 2:4], Q127, None,
                                               ALU.mult))
    t_c1c = rt_bcast(awsb[0:1, 4:6], c1col[:, 0:2], 2, waits=[("V", t_c12)])
    t_r1 = op("ve", lambda e: e.reciprocal(rows[0:1, 1024:2048], rows[0:1, 0:1024]),
              waits=[("G", t_g)])
    t_r1 = op("ve", lambda e: e.tensor_scalar(rows[0:1, 1024:2048], rows[0:1, 1024:2048], Q127,
                                              None, ALU.mult))
    t_s1 = op("ve", lambda e: e.tensor_scalar(rows[0:1, 2048:3072], rows[0:1, 0:1024],
                                              awsb[0:1, 0:1], QS, ALU.mult, ALU.mult))
    t_s1b = op("se", lambda e: e.copy(rowsb[0:1, 0:1024], rows[0:1, 2048:3072]),
               waits=[("V", t_s1)])
    t_r1bc = rt_bcast(rows[0:1, 1024:2048], r_bc[:, :], 1024,
                      waits=[("V", t_r1), ("T", t_am1_tp)])
    t_s1bc = rt_bcast(rowsb[0:1, 0:1024], smb[:, :], 1024, bf=True, waits=[("S", t_s1b)])
    bar()

    X1Q = PB[32:48]
    SQ = PB[0:16]
    RCP = PB[48:52]
    t_prev = 0
    for ki in range(FT):
        trl = dma("sy", xf32[:, :], xT[128 * ki:128 * (ki + 1), 0:PC],
                  waits=[("S", t_prev)])
        t_m = op("ve", lambda e: e.tensor_tensor(ftmp[0][:, :], xf32[:, :], mbc[:, :],
                                                 ALU.subtract),
                 waits=[("G", t_r1bc), ("L", trl)])
        t_m = op("ve", lambda e: e.tensor_tensor(ftmp[0][:, :], ftmp[0][:, :], rsbc[:, :],
                                                 ALU.mult))
        t_m = op("ve", lambda e: e.tensor_tensor(ftmp[0][:, :], ftmp[0][:, :], r_bc[:, :],
                                                 ALU.mult))
        t_m = op("ve", lambda e: e.tensor_scalar(ftmp[0][:, :], ftmp[0][:, :], Q127, -Q127,
                                                 ALU.min, ALU.max))
        t_i = op("gq", lambda e: e.tensor_copy(xi32[:, :], ftmp[0][:, :]), waits=[("V", t_m)])
        t_prev = op("se", lambda e, ki=ki: e.copy(X1Q[ki][:, :], xi32[:, :]),
                    waits=[("P", t_i)])
    t_x1q = t_prev
    last_wld = {}
    t_pww_rd = 0
    t_pwb_rd = {}
    t_acc = {}
    t_el_prev = {}
    t_nrd = 0
    t_rcast_rd = 0
    t_qe_prev = {}
    t_att = {}
    for mi in range(H):
        p = mi % 2
        tl = dma("sy", pww[:, :], wq[mi], waits=[("S", t_pww_rd)])
        last_wld[p] = op("se", lambda e, p=p: e.copy(pwb[p][:, :], pww[:, :]),
                         waits=[("L", tl), ("T", t_pwb_rd.get(p, 0))])
        t_pww_rd = last_wld[p]
        t_el = {}
        for cg in range(NCG):
            h0 = cg * 512
            b = 2 + cg
            for ki in range(FT):
                op("te", lambda e, ki=ki, p=p, h0=h0, b=b: e.matmul(
                    pp[b][:, :], pwb[p][:, 128 * ki:128 * (ki + 1)], XNO[ki][:, h0:h0 + 512],
                    start=(ki == 0), stop=(ki == FT - 1), skip_group_check=True),
                   waits=[("S", last_wld[p]), ("V", t_acc.get(cg, 0))] if ki == 0 else ())
            t_mm = C["T"]
            t_acc[cg] = op("ve", lambda e, b=b, mi=mi, h0=h0: e.tensor_scalar(
                ftmp[0][:, h0:h0 + 512], pp[b][:, :], bqs[:, mi:mi + 1], None, ALU.add),
                waits=[("T", t_mm), ("P", t_el_prev.get(cg, 0))])
            t_b = op("ve", lambda e, h0=h0: e.tensor_scalar(
                ftmp[1][:, h0:h0 + 512], ftmp[0][:, h0:h0 + 512], 0.0, None, ALU.min))
            t_e = op("se", lambda e, h0=h0: e.activation(
                ftmp[1][:, h0:h0 + 512], ftmp[1][:, h0:h0 + 512], AF.Exp), waits=[("V", t_b)])
            op("gq", lambda e, h0=h0: e.tensor_scalar(
                ftmp[0][:, h0:h0 + 512], ftmp[0][:, h0:h0 + 512], 0.0, None, ALU.max),
                waits=[("V", t_b)])
            t_el[cg] = op("gq", lambda e, mi=mi, h0=h0: e.tensor_tensor(
                SQ[mi][:, h0:h0 + 512], ftmp[0][:, h0:h0 + 512], ftmp[1][:, h0:h0 + 512], ALU.add),
                waits=[("S", t_e)])
            t_el_prev[cg] = t_el[cg]
        j = mi // 4
        for cg in range(NCG):
            h0 = cg * 512
            t_nm = op("te", lambda e, j=j, mi=mi, h0=h0: e.matmul(
                pp[6][0:1, :], kvb[:, 132 * j + 128:132 * j + 129], SQ[mi][:, h0:h0 + 512],
                start=True, stop=True, skip_group_check=True),
                waits=[("P", t_el[cg]), ("V", t_nrd)])
            op("ve", lambda e, h0=h0: e.tensor_scalar(rows[0:1, 3072 + h0:3072 + h0 + 512],
                                                      pp[6][0:1, :], EPS_AT, None, ALU.add),
               waits=[("T", t_nm), ("S", t_rcast_rd)])
            t_nrd = op("ve", lambda e, h0=h0: e.reciprocal(rows[0:1, 3072 + h0:3072 + h0 + 512],
                                                           rows[0:1, 3072 + h0:3072 + h0 + 512]))
        t_rcast_rd = op("se", lambda e: e.copy(rowsb[0:1, 0:1024], rows[0:1, 3072:4096]),
                        waits=[("V", t_nrd)])
        t_rcp = rt_bcast(rowsb[0:1, 0:1024], RCP[mi % 4][:, :], 1024, bf=True,
                         waits=[("S", t_rcast_rd), ("P", t_att.get(mi - 4, 0))])
        for cg in range(NCG):
            h0 = cg * 512
            b = 4 + cg
            op("te", lambda e, j=j, mi=mi, h0=h0, b=b: e.matmul(
                pp[b][:, :], kvb[:, 132 * j:132 * j + 128], SQ[mi][:, h0:h0 + 512],
                start=True, stop=True, skip_group_check=True),
               waits=[("S", t_qe_prev.get(cg, 0))])
            t_qk = C["T"]
            t_qe = op("se", lambda e, mi=mi, h0=h0, b=b: e.copy(
                SQ[mi][:, h0:h0 + 512], pp[b][:, :]), waits=[("T", t_qk)])
            t_qe_prev[cg] = t_qe
            t_att[mi] = op("gq", lambda e, mi=mi, h0=h0: e.tensor_tensor(
                SQ[mi][:, h0:h0 + 512], SQ[mi][:, h0:h0 + 512], RCP[mi % 4][:, h0:h0 + 512],
                ALU.mult), waits=[("S", t_qe), ("G", t_rcp)])
        t_pwb_rd[p] = t_mm
    bar()
    AO = PB[16:32]
    last_wld = {}
    t_pww_rd = 0
    t_pwb_rd = {}
    t_evac = {}
    for mi in range(FT):
        p = mi % 2
        tl = dma("sy", pww[:, :], wo[mi], waits=[("S", t_pww_rd)])
        last_wld[p] = op("se", lambda e, p=p: e.copy(pwb[p][:, :], pww[:, :]),
                         waits=[("L", tl), ("T", t_pwb_rd.get(p, 0))])
        t_pww_rd = last_wld[p]
        for cg in range(NCG):
            h0 = cg * 512
            b = cg
            for ki in range(FT):
                op("te", lambda e, ki=ki, p=p, h0=h0, b=b: e.matmul(
                    pp[b][:, :], pwb[p][:, 128 * ki:128 * (ki + 1)], SQ[ki][:, h0:h0 + 512],
                    start=(ki == 0), stop=(ki == FT - 1), skip_group_check=True),
                   waits=[("S", last_wld[p]), ("V", t_evac.get(cg, 0))] if ki == 0 else ())
            t_mm = C["T"]
            t_evac[cg] = op("ve", lambda e, mi=mi, h0=h0, b=b: e.tensor_copy(
                AO[mi][:, h0:h0 + 512], pp[b][:, :]), waits=[("T", t_mm)])
        dma("gq", d_at[mi], AO[mi][:, :], waits=[("V", t_evac[1])])
        t_pwb_rd[p] = t_mm
    bar()

    X2S = PB[0:32] + PB[48:62]
    op("ve", lambda e: e.memset(mbc[:, :], -1e30))
    t_aminit = op("ve", lambda e: e.memset(rsbc[:, :], 1e30))
    last_q = {}
    t_pww_rd = 0
    t_pwb_rd = {}
    t_xi_rd = 0
    t_sc_prev = {}
    t_ge_prev = {}
    t_spill = {}
    t_am = 0
    for mi in range(HT):
        p = mi % 2
        tl = dma("sy", pww[:, :], w1[mi], waits=[("P", t_pww_rd)])
        t_casts = []
        for hf in range(2):
            h0 = hf * 1024
            t_q1 = op("ve", lambda e, h0=h0: e.tensor_scalar(
                ftmp[0][:, :], pww[:, h0:h0 + 1024], c1col[:, 0:1], None, ALU.mult),
                waits=[("L", tl), ("G", t_c1c), ("S", t_xi_rd)])
            t_q2 = op("gq", lambda e: e.tensor_copy(xi32[:, :], ftmp[0][:, :]), waits=[("V", t_q1)])
            t_xi_rd = op("se", lambda e, p=p, h0=h0: e.copy(pwb[p][:, h0:h0 + 1024],
                                                                   xi32[:, :]),
                         waits=[("P", t_q2), ("T", t_pwb_rd.get(p, 0)) if h0 == 0 else ("P", t_q2)])
            t_casts.append(t_xi_rd)
        last_q[p] = t_q2
        t_pww_rd = t_q2
        t_wq = t_casts[-1]
        if mi < X2_RES:
            dst = X2S[mi]
            w_dst = 0
        else:
            dst = X2S[44 + (mi % 2)]
            w_dst = t_spill.get(mi % 2, 0)
        for cg in range(NCG):
            h0 = cg * 512
            b = cg
            for ki in range(FT):
                op("te", lambda e, ki=ki, p=p, h0=h0, b=b: e.matmul(
                    pp[b][:, :], pwb[p][:, 128 * ki:128 * (ki + 1)], X1Q[ki][:, h0:h0 + 512],
                    start=(ki == 0), stop=(ki == FT - 1), skip_group_check=True),
                   waits=[("S", t_wq), ("V", t_sc_prev.get(cg, 0))] if ki == 0 else ())
            t_mm = C["T"]
            t_sc_prev[cg] = op("ve", lambda e, h0=h0, b=b: e.tensor_tensor(
                ftmp[1][:, h0:h0 + 512], pp[b][:, :], smb[:, h0:h0 + 512], ALU.mult),
                waits=[("T", t_mm), ("G", t_s1bc), ("S", t_ge_prev.get(cg, 0))])
            t_ge_prev[cg] = op("se", lambda e, mi=mi, h0=h0, dst=dst: e.activation(
                dst[:, h0:h0 + 512], ftmp[1][:, h0:h0 + 512], AF.Gelu_apprx_tanh,
                bias=b1s[:, mi:mi + 1], scale=1.0),
                waits=[("V", t_sc_prev[cg])] + ([("G", w_dst), ("V", t_aminit)] if cg == 0 else []))
            op("ve", lambda e, h0=h0, dst=dst: e.tensor_tensor(
                mbc[:, h0:h0 + 512], mbc[:, h0:h0 + 512], dst[:, h0:h0 + 512], ALU.max),
                waits=[("S", t_ge_prev[cg])])
            t_am_v = op("ve", lambda e, h0=h0, dst=dst: e.tensor_tensor(
                rsbc[:, h0:h0 + 512], rsbc[:, h0:h0 + 512], dst[:, h0:h0 + 512], ALU.min))
        t_pwb_rd[p] = t_mm
        if mi >= X2_RES:
            t_spill[mi % 2] = dma("gq", d_x2[mi - X2_RES], dst[:, :], waits=[("V", t_am_v)])
    bar()

    t_ng = op("ve", lambda e: e.tensor_scalar(r_bc[:, :], rsbc[:, :], -1.0, None, ALU.mult))
    t_ng = op("ve", lambda e: e.tensor_tensor(r_bc[:, :], r_bc[:, :], mbc[:, :], ALU.max))
    t_red = t_ng
    for u in range(8):
        t_t = op("te", lambda e, u=u: e.transpose(pp[0][:, 0:128], r_bc[:, 128 * u:128 * (u + 1)],
                                                  ids[:, :]),
                 waits=[("V", t_red)])
        t_red = op("ve", lambda e, u=u: e.tensor_reduce(amc[:, u:u + 1], pp[0][:, 0:128], AX.X,
                                                        ALU.max, apply_absolute_value=True),
                   waits=[("T", t_t)])
    t_am2_tp = C["T"]
    t_v = C["V"]
    t_t = op("te", lambda e: e.transpose(pp[1][0:8, 0:128], amc[:, 0:8], ids[:, :]),
             waits=[("V", t_v)])
    t_v = op("ve", lambda e: e.tensor_copy(tr8[:, :], pp[1][0:8, 0:128]), waits=[("T", t_t)])
    t_g = 0
    for u in range(8):
        t_g = dma("gq", rows[0:1, 128 * u:128 * (u + 1)], tr8[u:u + 1, :], waits=[("V", t_v)])
    t_r2 = op("ve", lambda e: e.reciprocal(rows[0:1, 1024:2048], rows[0:1, 0:1024]),
              waits=[("G", t_g)])
    t_r2 = op("ve", lambda e: e.tensor_scalar(rows[0:1, 1024:2048], rows[0:1, 1024:2048], Q127,
                                              None, ALU.mult))
    t_s2 = op("ve", lambda e: e.tensor_scalar(rows[0:1, 2048:3072], rows[0:1, 0:1024],
                                              awsb[0:1, 1:2], QS, ALU.mult, ALU.mult))
    t_s2b = op("se", lambda e: e.copy(rowsb[0:1, 0:1024], rows[0:1, 2048:3072]),
               waits=[("V", t_s2)])
    t_r2bc = rt_bcast(rows[0:1, 1024:2048], r_bc[:, :], 1024,
                      waits=[("V", t_r2), ("T", t_am2_tp)])
    t_s2bc = rt_bcast(rowsb[0:1, 0:1024], smb[:, :], 1024, bf=True, waits=[("S", t_s2b)])
    t_prev = 0
    for mi in range(X2_RES):
        t_m = op("ve", lambda e, mi=mi: e.tensor_tensor(ftmp[0][:, :], X2S[mi][:, :], r_bc[:, :],
                                                        ALU.mult),
                 waits=[("G", t_r2bc), ("S", t_prev)])
        t_m = op("ve", lambda e: e.tensor_scalar(ftmp[0][:, :], ftmp[0][:, :], Q127, -Q127,
                                                 ALU.min, ALU.max))
        t_i = op("gq", lambda e: e.tensor_copy(xi32[:, :], ftmp[0][:, :]), waits=[("V", t_m)])
        t_prev = op("se", lambda e, mi=mi: e.copy(X2S[mi][:, :], xi32[:, :]),
                    waits=[("P", t_i)])
    t_x2q = t_prev
    AOR = PB[32:48]
    t_aor = {}
    for mi in range(FT):
        t_aor[mi] = dma("sy", AOR[mi][:, :], d_at[mi], chain=(mi == 0))
    t_aor_all = t_aor[FT - 1]
    bar()

    STG = [X2S[44], X2S[45]]
    last_q = {}
    t_pww_rd = 0
    t_pwb_rd = {}
    t_stg_rd = {}
    last_stg = {}
    t_xi_rd = 0
    t_evac_v = 0
    t_evac_g = 0
    t_ydma = 0
    for b in range(4):
        for kg in range(16):
            p = kg % 2
            for kl in range(4):
                tl = dma("sy", pww[:, kl * 512:(kl + 1) * 512], w2[b, :, kg * 4 + kl, :],
                         waits=[("P", t_pww_rd), ("V", t_evac_v), ("P", t_evac_g)]
                         if kl == 0 else (), chain=(kl == 0))
            for hf in range(2):
                h0 = hf * 1024
                t_q1 = op("ve", lambda e, h0=h0: e.tensor_scalar(
                    ftmp[0][:, :], pww[:, h0:h0 + 1024], c1col[:, 1:2], None, ALU.mult),
                    waits=[("L", tl), ("S", t_xi_rd)])
                t_q2 = op("gq", lambda e: e.tensor_copy(xi32[:, :], ftmp[0][:, :]),
                          waits=[("V", t_q1)])
                t_xi_rd = op("se", lambda e, p=p, h0=h0: e.copy(
                    pwb[p][:, h0:h0 + 1024], xi32[:, :]),
                    waits=[("P", t_q2), ("T", t_pwb_rd.get(p, 0)) if h0 == 0 else ("P", t_q2)])
            last_q[p] = t_q2
            t_pww_rd = t_q2
            t_wq = t_xi_rd
            for kl in range(4):
                ki = kg * 4 + kl
                if ki < X2_RES:
                    rhs_t = X2S[ki]
                    wrhs = [("S", t_x2q)]
                else:
                    sp = ki % 2
                    trl = dma("sy", STG[sp][:, :], d_x2[ki - X2_RES],
                              waits=[("S", last_stg.get(sp, 0)), ("T", t_stg_rd.get(sp, 0))])
                    t_m = op("ve", lambda e, sp=sp: e.tensor_tensor(
                        ftmp[1][:, :], STG[sp][:, :], r_bc[:, :], ALU.mult),
                        waits=[("L", trl), ("G", t_ydma), ("S", t_xi_rd)])
                    t_m = op("ve", lambda e: e.tensor_scalar(ftmp[1][:, :], ftmp[1][:, :], Q127,
                                                             -Q127, ALU.min, ALU.max))
                    t_i = op("gq", lambda e: e.tensor_copy(xi32[:, :], ftmp[1][:, :]),
                             waits=[("V", t_m), ("S", t_xi_rd)])
                    t_xi_rd = op("se", lambda e, sp=sp: e.copy(STG[sp][:, :], xi32[:, :]),
                                 waits=[("P", t_i)])
                    last_stg[sp] = t_xi_rd
                    rhs_t = STG[sp]
                    wrhs = [("S", last_stg[sp])]
                for r in range(4):
                    for cg in range(NCG):
                        h0 = cg * 512
                        op("te", lambda e, r=r, cg=cg, kl=kl, p=p, rhs_t=rhs_t, h0=h0, ki=ki:
                           e.matmul(pp[2 * r + cg][:, :],
                                    pwb[p][:, kl * 512 + 128 * r:kl * 512 + 128 * r + 128],
                                    rhs_t[:, h0:h0 + 512],
                                    start=(ki == 0), stop=(ki == HT - 1), skip_group_check=True),
                           waits=(wrhs + [("S", t_wq)]) if (r == 0 and cg == 0) else ())
                if ki >= X2_RES:
                    t_stg_rd[ki % 2] = C["T"]
            t_pwb_rd[p] = C["T"]
        t_mmend = C["T"]
        t_4prev = 0
        for r in range(4):
            mi = 4 * b + r
            tlx = dma("sy", xf32[:, :], xT[128 * mi:128 * (mi + 1), 0:PC],
                      waits=[("V", t_evac_v), ("P", t_evac_g), ("P", t_4prev)])
            for cg in range(NCG):
                h0 = cg * 512
                t_1 = op("ve", lambda e, r=r, cg=cg, h0=h0: e.tensor_tensor(
                    ftmp[0][:, h0:h0 + 512], pp[2 * r + cg][:, :], smb[:, h0:h0 + 512], ALU.mult),
                    waits=[("T", t_mmend), ("G", t_s2bc), ("G", t_ydma)])
                t_2 = op("gq", lambda e, mi=mi, h0=h0: e.tensor_scalar(
                    ftmp[0][:, h0:h0 + 512], ftmp[0][:, h0:h0 + 512], b2s[:, mi:mi + 1], None,
                    ALU.add), waits=[("V", t_1)])
                t_3 = op("ve", lambda e, mi=mi, h0=h0: e.tensor_tensor(
                    ftmp[0][:, h0:h0 + 512], ftmp[0][:, h0:h0 + 512], AOR[mi][:, h0:h0 + 512],
                    ALU.add), waits=[("P", t_2), ("L", t_aor_all)])
                t_4 = op("gq", lambda e, mi=mi, h0=h0: e.tensor_tensor(
                    ftmp[1][:, h0:h0 + 512], ftmp[0][:, h0:h0 + 512], xf32[:, h0:h0 + 512],
                    ALU.add), waits=[("V", t_3), ("L", tlx), ("G", t_ydma)])
                t_ydma = dma("gq", y[128 * mi:128 * (mi + 1), h0:h0 + 512],
                             ftmp[1][:, h0:h0 + 512], waits=[("P", t_4)])
                t_4prev = t_4
        t_evac_v = C["V"]
        t_evac_g = C["P"]
    final_g = C["G"]
    cmds["gq"].append(lambda e, v=final_g: e.wait_ge(sems["G"], v))

    with es:
        block = es.enter_context(nc.Block())

        @block.sync
        def _(e):
            for f in cmds["sy"]:
                f(e)

        @block.vector
        def _(e):
            for f in cmds["ve"]:
                f(e)

        @block.scalar
        def _(e):
            for f in cmds["se"]:
                f(e)

        @block.gpsimd
        def _(e):
            for f in cmds["gq"]:
                f(e)

        @block.tensor
        def _(e):
            for f in cmds["te"]:
                f(e)

    return nc


def kernel(L0_attn_hidden_states, L0_attn_Wq, L0_attn_bq, L0_attn_Wk, L0_attn_bk,
           L0_attn_Wv, L0_attn_bv, L0_attn_Wo, L0_ffn_W1, L0_ffn_B1, L0_ffn_W2, L0_ffn_B2):
    f = np.float32
    x = np.asarray(L0_attn_hidden_states, f)[0]
    xTf = np.ascontiguousarray(x.T)
    wq_h = np.ascontiguousarray(np.asarray(L0_attn_Wq, f).reshape(16, 128, 16, 128)
                                .transpose(0, 3, 2, 1).reshape(16, 128, 2048))
    wo_h = np.ascontiguousarray(np.asarray(L0_attn_Wo, f).reshape(16, 128, 16, 128)
                                .transpose(0, 3, 2, 1).reshape(16, 128, 2048))
    w1_h = np.ascontiguousarray(np.asarray(L0_ffn_W1, f).reshape(64, 128, 16, 128)
                                .transpose(0, 3, 2, 1).reshape(64, 128, 2048))
    w2_h = np.ascontiguousarray(np.asarray(L0_ffn_W2, f).reshape(4, 4, 128, 64, 128)
                                .transpose(0, 4, 3, 1, 2).reshape(4, 128, 64, 512))
    wk_h = np.ascontiguousarray(np.asarray(L0_attn_Wk, f).T.reshape(16, 128, 512))
    wv_h = np.ascontiguousarray(np.asarray(L0_attn_Wv, f).T.reshape(16, 128, 512))
    bq_h = np.ascontiguousarray(np.asarray(L0_attn_bq, f).reshape(16, 128).T)
    bkv_h = np.ascontiguousarray(
        np.concatenate([np.asarray(L0_attn_bk, f), np.asarray(L0_attn_bv, f)])[None, :])
    b1_h = np.ascontiguousarray(np.asarray(L0_ffn_B1, f).reshape(64, 128).T)
    b2_h = np.ascontiguousarray(np.asarray(L0_ffn_B2, f).reshape(16, 128).T)
    idn = np.eye(128, dtype=f)

    nc = build_nc()
    in_maps = []
    for c in range(NCORE):
        xTc = np.ascontiguousarray(np.roll(xTf, -PC * c, axis=1))
        in_maps.append(dict(xT=xTc, wq=wq_h, wo=wo_h, w1=w1_h, w2=w2_h, wk=wk_h, wv=wv_h,
                            bqh=bq_h, bkv=bkv_h, b1h=b1_h, b2h=b2_h, idn=idn))
    res = run_bass_kernel_spmd(nc, in_maps, core_ids=list(range(NCORE)))
    out = np.empty((S, D), f)
    for c in range(NCORE):
        out[PC * c:PC * (c + 1), :] = res.results[c]["y"].T
    return out.reshape(1, S, D)


# version 06
# speedup vs baseline: 1.2787x; 1.2787x over previous
import numpy as np
from contextlib import ExitStack

import concourse.bass as bass
import concourse.mybir as mybir
from concourse.bass_utils import run_bass_kernel_spmd

F32 = mybir.dt.float32
BF16 = mybir.dt.bfloat16
I32 = mybir.dt.int32
AF = mybir.ActivationFunctionType
ALU = mybir.AluOpType
AX = mybir.AxisListType

NCORE = 8
S = 8192
PC = 1024
D = 2048
H = 16
KVH = 4
FT = D // 128
HT = 8192 // 128
NT = S // PC
TT = PC // 128
NCG = 2
Q127 = 127.0
QS = 1.0 / (127.0 * 127.0)
X2_RES = 44
EPS_LN = 1e-5
EPS_AT = 1e-6


def build_nc():
    nc = bass.Bass("TRN2", target_bir_lowering=False)

    xT = nc.dram_tensor("xT", [D, S], F32, kind="ExternalInput").ap()
    wq = nc.dram_tensor("wq", [H, 128, D], F32, kind="ExternalInput").ap()
    wo = nc.dram_tensor("wo", [FT, 128, D], F32, kind="ExternalInput").ap()
    w1 = nc.dram_tensor("w1", [HT, 128, D], F32, kind="ExternalInput").ap()
    w2 = nc.dram_tensor("w2", [4, 128, HT, 512], F32, kind="ExternalInput").ap()
    wk = nc.dram_tensor("wk", [FT, 128, 512], F32, kind="ExternalInput").ap()
    wv = nc.dram_tensor("wv", [FT, 128, 512], F32, kind="ExternalInput").ap()
    bqh = nc.dram_tensor("bqh", [128, H], F32, kind="ExternalInput").ap()
    bkv = nc.dram_tensor("bkv", [1, 1024], F32, kind="ExternalInput").ap()
    b1h = nc.dram_tensor("b1h", [128, HT], F32, kind="ExternalInput").ap()
    b2h = nc.dram_tensor("b2h", [128, FT], F32, kind="ExternalInput").ap()
    idn = nc.dram_tensor("idn", [128, 128], F32, kind="ExternalInput").ap()
    y = nc.dram_tensor("y", [D, PC], F32, kind="ExternalOutput").ap()

    d_bc = nc.dram_tensor("d_bc", [8, 2048], F32, kind="Internal").ap()
    d_bcb = nc.dram_tensor("d_bcb", [8, 2048], BF16, kind="Internal").ap()
    d_at = nc.dram_tensor("d_at", [FT, 128, PC], BF16, kind="Internal").ap()
    d_x2 = nc.dram_tensor("d_x2", [HT - X2_RES, 128, PC], BF16, kind="Internal").ap()

    es = ExitStack()
    sbuf = lambda n, sh, dt: es.enter_context(nc.sbuf_tensor(n, sh, dt))
    PB = [sbuf(f"pb{i}", [128, PC], BF16) for i in range(62)]
    ftmp = [sbuf(f"ft{i}", [128, PC], F32) for i in range(2)]
    xf32 = sbuf("xf32", [128, PC], F32)
    r_bc = sbuf("r_bc", [128, PC], F32)
    pww = sbuf("pww", [128, D], F32)
    pwb = [sbuf(f"pwb{i}", [128, D], BF16) for i in range(2)]
    xi32 = sbuf("xi32", [128, PC], I32)
    rows = sbuf("rows", [1, 4096], F32)
    rowsb = sbuf("rowsb", [1, 2048], BF16)
    kvb = sbuf("kvb", [128, 4 * 132], BF16)
    vext = sbuf("vext", [128, 4 * 132], BF16)
    skb = sbuf("skb", [128, 512], BF16)
    amc = sbuf("amc", [128, 16], F32)
    tr8 = sbuf("tr8", [8, 128], F32)
    bqs = sbuf("bqs", [128, H], F32)
    b1s = sbuf("b1s", [128, HT], F32)
    b2s = sbuf("b2s", [128, FT], F32)
    ids = sbuf("ids", [128, 128], F32)
    onesb = sbuf("onesb", [128, 1], BF16)
    bkvb = sbuf("bkvb", [128, 1024], F32)
    awsb = sbuf("awsb", [1, 8], F32)
    c1col = sbuf("c1col", [128, 2], F32)
    smb = sbuf("smb", [128, PC], BF16)
    mbc = sbuf("mbc", [128, PC], BF16)
    rsbc = sbuf("rsbc", [128, PC], BF16)

    pp = [es.enter_context(nc.psum_tensor(f"pp{i}", [128, 512], F32)) for i in range(8)]

    sems = {k: es.enter_context(nc.semaphore(f"s{k}")) for k in "LGVSTP"}
    C = {k: 0 for k in "LGVSTP"}
    cmds = {"sy": [], "ve": [], "se": [], "gq": [], "te": []}
    QSEM = {"sy": "L", "ve": "V", "se": "S", "gq": "P", "te": "T"}

    def op(q, fn, waits=()):
        sk = QSEM[q]
        w = list(waits) + ([] if q == "te" else [(sk, C[sk])])
        C[sk] += 1
        w = tuple((k, v) for k, v in w if v)

        def run(e, fn=fn, w=w, sem=sems[sk]):
            for k, v in w:
                e.wait_ge(sems[k], v)
            fn(e).then_inc(sem, 1)
        cmds[q].append(run)
        return C[sk]

    def dma(q, out, in_, waits=(), chain=True):
        sk = "L" if q == "sy" else "G"
        w = list(waits)
        if chain:
            w.append((sk, C[sk]))
        C[sk] += 16
        w = tuple((k, v) for k, v in w if v)

        def run(e, out=out, in_=in_, w=w, sem=sems[sk]):
            for k, v in w:
                e.wait_ge(sems[k], v)
            e.dma_start(out, in_).then_inc(sem, 16)
        cmds[q].append(run)
        return C[sk]

    def bar():
        snap = dict(C)
        for q in cmds:
            def run(e, snap=snap):
                for k, v in snap.items():
                    if v:
                        e.wait_ge(sems[k], v)
            cmds[q].append(run)

    bcslot = [0]
    bc_rd = {}

    def rt_bcast(row_ap, out_ap, n, bf=False, waits=()):
        sl = bcslot[0] % 8
        bcslot[0] += 1
        dd = d_bcb if bf else d_bc
        w = list(waits) + ([("G", bc_rd[sl])] if sl in bc_rd else [])
        t1 = dma("gq", dd[sl:sl + 1, 0:n], row_ap, waits=w)
        t2 = dma("gq", out_ap, dd[sl:sl + 1, 0:n].broadcast_to([128, n]), waits=[("G", t1)])
        bc_rd[sl] = t2
        return t2

    dma("sy", bqs[:, :], bqh[:, :])
    dma("sy", b1s[:, :], b1h[:, :])
    dma("sy", b2s[:, :], b2h[:, :])
    dma("sy", ids[:, :], idn[:, :])
    op("ve", lambda e: e.memset(onesb[:, :], 1.0))
    for j in range(KVH):
        op("ve", lambda e, j=j: e.memset(vext[:, j * 132 + 128:j * 132 + 129], 1.0))
    op("ve", lambda e: e.memset(amc[:, 9:10], 0.0))
    op("ve", lambda e: e.memset(amc[:, 10:11], 0.0))
    dma("gq", bkvb[:, :], bkv[0:1, :].broadcast_to([128, 1024]))

    WKB = PB[46:54]
    WVB = PB[54:62]
    last_cast = {}
    for u in range(32):
        src = wk if u < 16 else wv
        ki = u % 16
        slot = u % 4
        w = [("S", last_cast.get(slot, 0))]
        tl = dma("sy", pww[:, slot * 512:(slot + 1) * 512], src[ki], waits=w)
        dstp = (WKB if u < 16 else WVB)[ki // 2]
        half = (ki % 2) * 512
        last_cast[slot] = op(
            "se", lambda e, s=slot, dp=dstp, h=half:
            e.copy(dp[:, h:h + 512], pww[:, s * 512:(s + 1) * 512]),
            waits=[("L", tl)])
    bar()

    XB = PB[0:16]
    XNO = PB[16:32]
    SQR = PB[32:34]
    wchunk = [0]
    last_wred = [0]

    def emit_wchunk():
        i = wchunk[0]
        if i >= 128:
            return
        wchunk[0] += 1
        if i < 64:
            tl = dma("sy", pww[:, :], w1[i], waits=[("V", last_wred[0])])
        else:
            j = i - 64
            b, kg = j // 16, j % 16
            for kl in range(4):
                tl = dma("sy", pww[:, kl * 512:(kl + 1) * 512], w2[b, :, kg * 4 + kl, :],
                         waits=[("V", last_wred[0])] if kl == 0 else (), chain=(kl == 0))
        t_red = op("ve", lambda e: e.tensor_reduce(amc[:, 8:9], pww[:, :], AX.X, ALU.max,
                                                   apply_absolute_value=True), waits=[("L", tl)])
        dst = 9 if i < 64 else 10
        last_wred[0] = op("ve", lambda e, d=dst: e.tensor_tensor(
            amc[:, d:d + 1], amc[:, d:d + 1], amc[:, 8:9], ALU.max), waits=[("V", t_red)])

    t_amacc = 0
    t_ordr = list(range(1, NT)) + [0]
    for idx, T in enumerate(t_ordr):
        first, last = idx == 0, idx == NT - 1
        xsrc = XNO if T == 0 else XB
        t_sq = {}
        t_cast = {}
        t_smm = {}
        for ki in range(FT):
            w = []
            if ki > 0:
                w = [("S", t_sq[ki - 1]), ("V", t_cast[ki - 1])]
            tld = dma("sy", xf32[:, :], xT[128 * ki:128 * (ki + 1), PC * T:PC * (T + 1)], waits=w)
            t_sq[ki] = op("se", lambda e, ki=ki: e.activation(SQR[ki % 2][:, :], xf32[:, :],
                                                              AF.Square),
                          waits=[("L", tld), ("T", t_smm.get(ki - 2, 0))])
            t_cast[ki] = op("ve", lambda e, ki=ki: e.tensor_copy(XB[ki][:, :], xf32[:, :]),
                            waits=[("L", tld)])
            op("te", lambda e, ki=ki: e.matmul(pp[0][0:1, :], onesb[:, :], XB[ki][:, 0:512],
                                               start=(ki == 0), stop=(ki == FT - 1),
                                               skip_group_check=True),
               waits=[("V", t_cast[ki])])
            op("te", lambda e, ki=ki: e.matmul(pp[2][0:1, :], onesb[:, :], XB[ki][:, 512:1024],
                                               start=(ki == 0), stop=(ki == FT - 1),
                                               skip_group_check=True))
            op("te", lambda e, ki=ki: e.matmul(pp[1][0:1, :], onesb[:, :], SQR[ki % 2][:, 0:512],
                                               start=(ki == 0), stop=(ki == FT - 1),
                                               skip_group_check=True),
               waits=[("S", t_sq[ki])])
            t_smm[ki] = op("te", lambda e, ki=ki: e.matmul(pp[3][0:1, :], onesb[:, :],
                                                           SQR[ki % 2][:, 512:1024],
                                                           start=(ki == 0), stop=(ki == FT - 1),
                                                           skip_group_check=True))
        for _ in range(16):
            emit_wchunk()
        t_mm = C["T"]
        t_stat = 0
        for cg in range(NCG):
            h0 = cg * 512
            ps, pq = (0, 1) if cg == 0 else (2, 3)
            op("ve", lambda e, h0=h0, ps=ps: e.tensor_scalar(rows[0:1, h0:h0 + 512],
                                                             pp[ps][0:1, :], 1.0 / D, None,
                                                             ALU.mult),
               waits=[("T", t_mm)])
            op("ve", lambda e, h0=h0, pq=pq: e.tensor_scalar(rows[0:1, 1024 + h0:1024 + h0 + 512],
                                                             pp[pq][0:1, :], 1.0 / D, None,
                                                             ALU.mult))
            op("ve", lambda e, h0=h0: e.tensor_tensor(rows[0:1, 2048 + h0:2048 + h0 + 512],
                                                      rows[0:1, h0:h0 + 512],
                                                      rows[0:1, h0:h0 + 512], ALU.mult))
            t_stat = op("ve", lambda e, h0=h0: e.tensor_tensor(
                rows[0:1, 2048 + h0:2048 + h0 + 512], rows[0:1, 1024 + h0:1024 + h0 + 512],
                rows[0:1, 2048 + h0:2048 + h0 + 512], ALU.subtract))
        t_ve = op("ve", lambda e: e.tensor_scalar(rows[0:1, 2048:3072], rows[0:1, 2048:3072],
                                                  EPS_LN, None, ALU.add))
        t_sq2 = op("se", lambda e: e.activation(rows[0:1, 2048:3072], rows[0:1, 2048:3072],
                                                AF.Sqrt), waits=[("V", t_ve)])
        t_rstd = op("ve", lambda e: e.reciprocal(rows[0:1, 2048:3072], rows[0:1, 2048:3072]),
                    waits=[("S", t_sq2)])
        t_mrow = op("se", lambda e: e.copy(rowsb[0:1, 0:1024], rows[0:1, 0:1024]))
        t_rrow = op("se", lambda e: e.copy(rowsb[0:1, 1024:2048], rows[0:1, 2048:3072]),
                    waits=[("V", t_rstd)])
        rt_bcast(rowsb[0:1, 0:1024], mbc[:, :], 1024, bf=True, waits=[("S", t_mrow)])
        t_rb = rt_bcast(rowsb[0:1, 1024:2048], rsbc[:, :], 1024, bf=True, waits=[("S", t_rrow)])
        if T == 0:
            op("ve", lambda e: e.memset(r_bc[:, :], 0.0))
        t_xn = {}
        t_prev_g = 0
        t_prev_s = 0
        for ki in range(FT):
            if T == 0:
                trl = dma("sy", xf32[:, :], xT[128 * ki:128 * (ki + 1), 0:PC],
                          waits=[("S", t_prev_s), ("V", t_amacc),
                                 ("S", t_sq[FT - 1]), ("V", t_cast[FT - 1])])
                t_c = op("ve", lambda e: e.tensor_tensor(ftmp[0][:, :], xf32[:, :], mbc[:, :],
                                                         ALU.subtract),
                         waits=[("G", t_rb), ("L", trl)])
                t_c2 = op("ve", lambda e: e.tensor_tensor(ftmp[1][:, :], ftmp[0][:, :],
                                                          rsbc[:, :], ALU.mult))
                t_prev_s = op("se", lambda e, ki=ki: e.copy(XNO[ki][:, :], ftmp[1][:, :]),
                              waits=[("V", t_c2)])
                t_ab = op("se", lambda e: e.activation(ftmp[0][:, :], ftmp[1][:, :], AF.Abs))
                t_amacc = op("ve", lambda e: e.tensor_tensor(
                    r_bc[:, :], r_bc[:, :], ftmp[0][:, :], ALU.max), waits=[("S", t_ab)])
                t_xn[ki] = t_prev_s
            else:
                t_c = op("ve", lambda e, ki=ki: e.tensor_tensor(ftmp[0][:, :], XB[ki][:, :],
                                                                mbc[:, :], ALU.subtract),
                         waits=[("G", t_rb), ("P", t_prev_g)])
                t_prev_g = op("gq", lambda e, dst=xsrc[ki]: e.tensor_tensor(
                    dst[:, :], ftmp[0][:, :], rsbc[:, :], ALU.mult), waits=[("V", t_c)])
                t_xn[ki] = t_prev_g
        t_kv_rd = t_stat
        t_sk_prev = 0
        t_kvmm = 0
        xnsem = "S" if T == 0 else "P"
        for tt in range(TT):
            for ki in range(FT):
                op("te", lambda e, ki=ki, tt=tt, xs=xsrc: e.matmul(
                    pp[2][:, :], xs[ki][:, 128 * tt:128 * (tt + 1)],
                    WKB[ki // 2][:, (ki % 2) * 512:(ki % 2) * 512 + 512],
                    start=(ki == 0), stop=(ki == FT - 1), skip_group_check=True),
                   waits=[(xnsem, t_xn[ki])] + ([("V", t_kv_rd), ("P", t_sk_prev)]
                                               if ki == 0 else []))
                op("te", lambda e, ki=ki, tt=tt, xs=xsrc: e.matmul(
                    pp[3][:, :], xs[ki][:, 128 * tt:128 * (tt + 1)],
                    WVB[ki // 2][:, (ki % 2) * 512:(ki % 2) * 512 + 512],
                    start=(ki == 0), stop=(ki == FT - 1), skip_group_check=True))
            t_kp = C["T"]
            t_a = op("ve", lambda e: e.tensor_tensor(ftmp[0][:, 0:512], pp[2][:, :], bkvb[:, 0:512],
                                                     ALU.add),
                     waits=[("T", t_kp), ("P", t_sk_prev)])
            t_b = op("ve", lambda e: e.tensor_scalar(ftmp[1][:, 0:512], ftmp[0][:, 0:512], 0.0,
                                                     None, ALU.min))
            t_e = op("se", lambda e: e.activation(ftmp[1][:, 0:512], ftmp[1][:, 0:512], AF.Exp),
                     waits=[("V", t_b)])
            t_m = op("gq", lambda e: e.tensor_scalar(ftmp[0][:, 0:512], ftmp[0][:, 0:512], 0.0,
                                                     None, ALU.max),
                     waits=[("V", t_b), ("T", t_kvmm)])
            t_sk = op("gq", lambda e: e.tensor_tensor(skb[:, :], ftmp[0][:, 0:512],
                                                      ftmp[1][:, 0:512], ALU.add),
                      waits=[("S", t_e)])
            t_vx = 0
            for j in range(KVH):
                t_vx = op("ve", lambda e, j=j: e.tensor_tensor(
                    vext[:, 132 * j:132 * j + 128], pp[3][:, 128 * j:128 * (j + 1)],
                    bkvb[:, 512 + 128 * j:512 + 128 * (j + 1)], ALU.add),
                    waits=[("T", t_kp)] + ([("T", t_kvmm)] if j == 0 else []))
            for j in range(KVH):
                op("te", lambda e, j=j, st=(first and tt == 0), sp=(last and tt == TT - 1):
                   e.matmul(
                    pp[4 + j][:, 0:129], skb[:, 128 * j:128 * (j + 1)],
                    vext[:, 132 * j:132 * j + 129],
                    start=st, stop=sp, skip_group_check=True),
                   waits=[("P", t_sk), ("V", t_vx)] if j == 0 else ())
            t_kvmm = C["T"]
            t_kv_rd = max(t_a, t_vx)
            t_sk_prev = t_sk
        bar()
    while wchunk[0] < 128:
        emit_wchunk()
    bar()

    for j in range(KVH):
        op("ve", lambda e, j=j: e.tensor_copy(kvb[:, 132 * j:132 * j + 129], pp[4 + j][:, 0:129]))
    t_red = C["V"]
    for u in range(8):
        t_t = op("te", lambda e, u=u: e.transpose(pp[0][:, 0:128], r_bc[:, 128 * u:128 * (u + 1)],
                                                  ids[:, :]),
                 waits=[("V", t_red)])
        t_red = op("ve", lambda e, u=u: e.tensor_reduce(amc[:, u:u + 1], pp[0][:, 0:128], AX.X,
                                                        ALU.max, apply_absolute_value=True),
                   waits=[("T", t_t)])
    t_am1_tp = C["T"]
    t_v = C["V"]
    t_t = op("te", lambda e: e.transpose(pp[1][0:8, 0:128], amc[:, 0:8], ids[:, :]),
             waits=[("V", t_v)])
    t_v = op("ve", lambda e: e.tensor_copy(tr8[:, :], pp[1][0:8, 0:128]), waits=[("T", t_t)])
    t_g = 0
    for u in range(8):
        t_g = dma("gq", rows[0:1, 128 * u:128 * (u + 1)], tr8[u:u + 1, :], waits=[("V", t_v)])
    t_t = op("te", lambda e: e.transpose(pp[2][0:1, 0:128], amc[:, 9:10], ids[:, :]))
    t_t2 = op("te", lambda e: e.transpose(pp[3][0:1, 0:128], amc[:, 10:11], ids[:, :]))
    op("ve", lambda e: e.tensor_reduce(awsb[0:1, 0:1], pp[2][0:1, 0:128], AX.X, ALU.max,
                                       apply_absolute_value=True), waits=[("T", t_t2)])
    op("ve", lambda e: e.tensor_reduce(awsb[0:1, 1:2], pp[3][0:1, 0:128], AX.X, ALU.max,
                                       apply_absolute_value=True))
    op("ve", lambda e: e.reciprocal(awsb[0:1, 2:4], awsb[0:1, 0:2]))
    t_c12 = op("ve", lambda e: e.tensor_scalar(awsb[0:1, 4:6], awsb[0:1, 2:4], Q127, None,
                                               ALU.mult))
    t_c1c = rt_bcast(awsb[0:1, 4:6], c1col[:, 0:2], 2, waits=[("V", t_c12)])
    t_r1 = op("ve", lambda e: e.reciprocal(rows[0:1, 1024:2048], rows[0:1, 0:1024]),
              waits=[("G", t_g)])
    t_r1 = op("ve", lambda e: e.tensor_scalar(rows[0:1, 1024:2048], rows[0:1, 1024:2048], Q127,
                                              None, ALU.mult))
    t_s1 = op("ve", lambda e: e.tensor_scalar(rows[0:1, 2048:3072], rows[0:1, 0:1024],
                                              awsb[0:1, 0:1], QS, ALU.mult, ALU.mult))
    t_s1b = op("se", lambda e: e.copy(rowsb[0:1, 0:1024], rows[0:1, 2048:3072]),
               waits=[("V", t_s1)])
    t_r1bc = rt_bcast(rows[0:1, 1024:2048], r_bc[:, :], 1024,
                      waits=[("V", t_r1), ("T", t_am1_tp)])
    t_s1bc = rt_bcast(rowsb[0:1, 0:1024], smb[:, :], 1024, bf=True, waits=[("S", t_s1b)])
    bar()

    X1Q = PB[32:48]
    SQ = PB[0:16]
    RCP = PB[48:52]
    t_prev = 0
    for ki in range(FT):
        trl = dma("sy", xf32[:, :], xT[128 * ki:128 * (ki + 1), 0:PC],
                  waits=[("S", t_prev)])
        t_m = op("ve", lambda e: e.tensor_tensor(ftmp[0][:, :], xf32[:, :], mbc[:, :],
                                                 ALU.subtract),
                 waits=[("G", t_r1bc), ("L", trl)])
        t_m = op("ve", lambda e: e.tensor_tensor(ftmp[0][:, :], ftmp[0][:, :], rsbc[:, :],
                                                 ALU.mult))
        t_m = op("ve", lambda e: e.tensor_tensor(ftmp[0][:, :], ftmp[0][:, :], r_bc[:, :],
                                                 ALU.mult))
        t_m = op("ve", lambda e: e.tensor_scalar(ftmp[0][:, :], ftmp[0][:, :], Q127, -Q127,
                                                 ALU.min, ALU.max))
        t_i = op("gq", lambda e: e.tensor_copy(xi32[:, :], ftmp[0][:, :]), waits=[("V", t_m)])
        t_prev = op("se", lambda e, ki=ki: e.copy(X1Q[ki][:, :], xi32[:, :]),
                    waits=[("P", t_i)])
    t_x1q = t_prev
    last_wld = {}
    t_pww_rd = 0
    t_pwb_rd = {}
    t_acc = {}
    t_el_prev = {}
    t_nrd = 0
    t_rcast_rd = 0
    t_qe_prev = {}
    t_att = {}
    for mi in range(H):
        p = mi % 2
        tl = dma("sy", pww[:, :], wq[mi], waits=[("S", t_pww_rd)])
        last_wld[p] = op("se", lambda e, p=p: e.copy(pwb[p][:, :], pww[:, :]),
                         waits=[("L", tl), ("T", t_pwb_rd.get(p, 0))])
        t_pww_rd = last_wld[p]
        t_el = {}
        for cg in range(NCG):
            h0 = cg * 512
            b = 2 + cg
            for ki in range(FT):
                op("te", lambda e, ki=ki, p=p, h0=h0, b=b: e.matmul(
                    pp[b][:, :], pwb[p][:, 128 * ki:128 * (ki + 1)], XNO[ki][:, h0:h0 + 512],
                    start=(ki == 0), stop=(ki == FT - 1), skip_group_check=True),
                   waits=[("S", last_wld[p]), ("V", t_acc.get(cg, 0))] if ki == 0 else ())
            t_mm = C["T"]
            t_acc[cg] = op("ve", lambda e, b=b, mi=mi, h0=h0: e.tensor_scalar(
                ftmp[0][:, h0:h0 + 512], pp[b][:, :], bqs[:, mi:mi + 1], None, ALU.add),
                waits=[("T", t_mm), ("P", t_el_prev.get(cg, 0))])
            t_b = op("ve", lambda e, h0=h0: e.tensor_scalar(
                ftmp[1][:, h0:h0 + 512], ftmp[0][:, h0:h0 + 512], 0.0, None, ALU.min))
            t_e = op("se", lambda e, h0=h0: e.activation(
                ftmp[1][:, h0:h0 + 512], ftmp[1][:, h0:h0 + 512], AF.Exp), waits=[("V", t_b)])
            op("gq", lambda e, h0=h0: e.tensor_scalar(
                ftmp[0][:, h0:h0 + 512], ftmp[0][:, h0:h0 + 512], 0.0, None, ALU.max),
                waits=[("V", t_b)])
            t_el[cg] = op("gq", lambda e, mi=mi, h0=h0: e.tensor_tensor(
                SQ[mi][:, h0:h0 + 512], ftmp[0][:, h0:h0 + 512], ftmp[1][:, h0:h0 + 512], ALU.add),
                waits=[("S", t_e)])
            t_el_prev[cg] = t_el[cg]
        j = mi // 4
        for cg in range(NCG):
            h0 = cg * 512
            t_nm = op("te", lambda e, j=j, mi=mi, h0=h0: e.matmul(
                pp[6][0:1, :], kvb[:, 132 * j + 128:132 * j + 129], SQ[mi][:, h0:h0 + 512],
                start=True, stop=True, skip_group_check=True),
                waits=[("P", t_el[cg]), ("V", t_nrd)])
            op("ve", lambda e, h0=h0: e.tensor_scalar(rows[0:1, 3072 + h0:3072 + h0 + 512],
                                                      pp[6][0:1, :], EPS_AT, None, ALU.add),
               waits=[("T", t_nm), ("S", t_rcast_rd)])
            t_nrd = op("ve", lambda e, h0=h0: e.reciprocal(rows[0:1, 3072 + h0:3072 + h0 + 512],
                                                           rows[0:1, 3072 + h0:3072 + h0 + 512]))
        t_rcast_rd = op("se", lambda e: e.copy(rowsb[0:1, 0:1024], rows[0:1, 3072:4096]),
                        waits=[("V", t_nrd)])
        t_rcp = rt_bcast(rowsb[0:1, 0:1024], RCP[mi % 4][:, :], 1024, bf=True,
                         waits=[("S", t_rcast_rd), ("P", t_att.get(mi - 4, 0))])
        for cg in range(NCG):
            h0 = cg * 512
            b = 4 + cg
            op("te", lambda e, j=j, mi=mi, h0=h0, b=b: e.matmul(
                pp[b][:, :], kvb[:, 132 * j:132 * j + 128], SQ[mi][:, h0:h0 + 512],
                start=True, stop=True, skip_group_check=True),
               waits=[("S", t_qe_prev.get(cg, 0))])
            t_qk = C["T"]
            t_qe = op("se", lambda e, mi=mi, h0=h0, b=b: e.copy(
                SQ[mi][:, h0:h0 + 512], pp[b][:, :]), waits=[("T", t_qk)])
            t_qe_prev[cg] = t_qe
            t_att[mi] = op("gq", lambda e, mi=mi, h0=h0: e.tensor_tensor(
                SQ[mi][:, h0:h0 + 512], SQ[mi][:, h0:h0 + 512], RCP[mi % 4][:, h0:h0 + 512],
                ALU.mult), waits=[("S", t_qe), ("G", t_rcp)])
        t_pwb_rd[p] = t_mm
    bar()
    AO = PB[16:32]
    last_wld = {}
    t_pww_rd = 0
    t_pwb_rd = {}
    t_evac = {}
    for mi in range(FT):
        p = mi % 2
        tl = dma("sy", pww[:, :], wo[mi], waits=[("S", t_pww_rd)])
        last_wld[p] = op("se", lambda e, p=p: e.copy(pwb[p][:, :], pww[:, :]),
                         waits=[("L", tl), ("T", t_pwb_rd.get(p, 0))])
        t_pww_rd = last_wld[p]
        for cg in range(NCG):
            h0 = cg * 512
            b = cg
            for ki in range(FT):
                op("te", lambda e, ki=ki, p=p, h0=h0, b=b: e.matmul(
                    pp[b][:, :], pwb[p][:, 128 * ki:128 * (ki + 1)], SQ[ki][:, h0:h0 + 512],
                    start=(ki == 0), stop=(ki == FT - 1), skip_group_check=True),
                   waits=[("S", last_wld[p]), ("V", t_evac.get(cg, 0))] if ki == 0 else ())
            t_mm = C["T"]
            t_evac[cg] = op("ve", lambda e, mi=mi, h0=h0, b=b: e.tensor_copy(
                AO[mi][:, h0:h0 + 512], pp[b][:, :]), waits=[("T", t_mm)])
        dma("gq", d_at[mi], AO[mi][:, :], waits=[("V", t_evac[1])])
        t_pwb_rd[p] = t_mm
    bar()

    X2S = PB[0:32] + PB[48:62]
    op("ve", lambda e: e.memset(mbc[:, :], -1e30))
    t_aminit = op("ve", lambda e: e.memset(rsbc[:, :], 1e30))
    last_q = {}
    t_pww_rd = 0
    t_pwb_rd = {}
    t_xi_rd = 0
    t_sc_prev = {}
    t_ge_prev = {}
    t_spill = {}
    t_am = 0
    for mi in range(HT):
        p = mi % 2
        tl = dma("sy", pww[:, :], w1[mi], waits=[("P", t_pww_rd)])
        t_casts = []
        for hf in range(2):
            h0 = hf * 1024
            t_q1 = op("ve", lambda e, h0=h0: e.tensor_scalar(
                ftmp[0][:, :], pww[:, h0:h0 + 1024], c1col[:, 0:1], None, ALU.mult),
                waits=[("L", tl), ("G", t_c1c), ("S", t_xi_rd)])
            t_q2 = op("gq", lambda e: e.tensor_copy(xi32[:, :], ftmp[0][:, :]), waits=[("V", t_q1)])
            t_xi_rd = op("se", lambda e, p=p, h0=h0: e.copy(pwb[p][:, h0:h0 + 1024],
                                                                   xi32[:, :]),
                         waits=[("P", t_q2), ("T", t_pwb_rd.get(p, 0)) if h0 == 0 else ("P", t_q2)])
            t_casts.append(t_xi_rd)
        last_q[p] = t_q2
        t_pww_rd = t_q2
        t_wq = t_casts[-1]
        if mi < X2_RES:
            dst = X2S[mi]
            w_dst = 0
        else:
            dst = X2S[44 + (mi % 2)]
            w_dst = t_spill.get(mi % 2, 0)
        for cg in range(NCG):
            h0 = cg * 512
            b = cg
            for ki in range(FT):
                op("te", lambda e, ki=ki, p=p, h0=h0, b=b: e.matmul(
                    pp[b][:, :], pwb[p][:, 128 * ki:128 * (ki + 1)], X1Q[ki][:, h0:h0 + 512],
                    start=(ki == 0), stop=(ki == FT - 1), skip_group_check=True),
                   waits=[("S", t_wq), ("V", t_sc_prev.get(cg, 0))] if ki == 0 else ())
            t_mm = C["T"]
            t_sc_prev[cg] = op("ve", lambda e, h0=h0, b=b: e.tensor_tensor(
                ftmp[1][:, h0:h0 + 512], pp[b][:, :], smb[:, h0:h0 + 512], ALU.mult),
                waits=[("T", t_mm), ("G", t_s1bc), ("S", t_ge_prev.get(cg, 0))])
            t_ge_prev[cg] = op("se", lambda e, mi=mi, h0=h0, dst=dst: e.activation(
                dst[:, h0:h0 + 512], ftmp[1][:, h0:h0 + 512], AF.Gelu_apprx_tanh,
                bias=b1s[:, mi:mi + 1], scale=1.0),
                waits=[("V", t_sc_prev[cg])] + ([("G", w_dst), ("V", t_aminit)] if cg == 0 else []))
            op("ve", lambda e, h0=h0, dst=dst: e.tensor_tensor(
                mbc[:, h0:h0 + 512], mbc[:, h0:h0 + 512], dst[:, h0:h0 + 512], ALU.max),
                waits=[("S", t_ge_prev[cg])])
            t_am_v = op("ve", lambda e, h0=h0, dst=dst: e.tensor_tensor(
                rsbc[:, h0:h0 + 512], rsbc[:, h0:h0 + 512], dst[:, h0:h0 + 512], ALU.min))
        t_pwb_rd[p] = t_mm
        if mi >= X2_RES:
            t_spill[mi % 2] = dma("gq", d_x2[mi - X2_RES], dst[:, :], waits=[("V", t_am_v)])
    bar()

    t_ng = op("ve", lambda e: e.tensor_scalar(r_bc[:, :], rsbc[:, :], -1.0, None, ALU.mult))
    t_ng = op("ve", lambda e: e.tensor_tensor(r_bc[:, :], r_bc[:, :], mbc[:, :], ALU.max))
    t_red = t_ng
    for u in range(8):
        t_t = op("te", lambda e, u=u: e.transpose(pp[0][:, 0:128], r_bc[:, 128 * u:128 * (u + 1)],
                                                  ids[:, :]),
                 waits=[("V", t_red)])
        t_red = op("ve", lambda e, u=u: e.tensor_reduce(amc[:, u:u + 1], pp[0][:, 0:128], AX.X,
                                                        ALU.max, apply_absolute_value=True),
                   waits=[("T", t_t)])
    t_am2_tp = C["T"]
    t_v = C["V"]
    t_t = op("te", lambda e: e.transpose(pp[1][0:8, 0:128], amc[:, 0:8], ids[:, :]),
             waits=[("V", t_v)])
    t_v = op("ve", lambda e: e.tensor_copy(tr8[:, :], pp[1][0:8, 0:128]), waits=[("T", t_t)])
    t_g = 0
    for u in range(8):
        t_g = dma("gq", rows[0:1, 128 * u:128 * (u + 1)], tr8[u:u + 1, :], waits=[("V", t_v)])
    t_r2 = op("ve", lambda e: e.reciprocal(rows[0:1, 1024:2048], rows[0:1, 0:1024]),
              waits=[("G", t_g)])
    t_r2 = op("ve", lambda e: e.tensor_scalar(rows[0:1, 1024:2048], rows[0:1, 1024:2048], Q127,
                                              None, ALU.mult))
    t_s2 = op("ve", lambda e: e.tensor_scalar(rows[0:1, 2048:3072], rows[0:1, 0:1024],
                                              awsb[0:1, 1:2], QS, ALU.mult, ALU.mult))
    t_s2b = op("se", lambda e: e.copy(rowsb[0:1, 0:1024], rows[0:1, 2048:3072]),
               waits=[("V", t_s2)])
    t_r2bc = rt_bcast(rows[0:1, 1024:2048], r_bc[:, :], 1024,
                      waits=[("V", t_r2), ("T", t_am2_tp)])
    t_s2bc = rt_bcast(rowsb[0:1, 0:1024], smb[:, :], 1024, bf=True, waits=[("S", t_s2b)])
    t_prev = 0
    for mi in range(X2_RES):
        t_m = op("ve", lambda e, mi=mi: e.tensor_tensor(ftmp[0][:, :], X2S[mi][:, :], r_bc[:, :],
                                                        ALU.mult),
                 waits=[("G", t_r2bc), ("S", t_prev)])
        t_m = op("ve", lambda e: e.tensor_scalar(ftmp[0][:, :], ftmp[0][:, :], Q127, -Q127,
                                                 ALU.min, ALU.max))
        t_i = op("gq", lambda e: e.tensor_copy(xi32[:, :], ftmp[0][:, :]), waits=[("V", t_m)])
        t_prev = op("se", lambda e, mi=mi: e.copy(X2S[mi][:, :], xi32[:, :]),
                    waits=[("P", t_i)])
    t_x2q = t_prev
    AOR = PB[32:48]
    t_aor = {}
    for mi in range(FT):
        t_aor[mi] = dma("sy", AOR[mi][:, :], d_at[mi], chain=(mi == 0))
    t_aor_all = t_aor[FT - 1]
    bar()

    STG = [X2S[44], X2S[45]]
    last_q = {}
    t_pww_rd = 0
    t_pwb_rd = {}
    t_stg_rd = {}
    last_stg = {}
    t_xi_rd = 0
    t_evac_v = 0
    t_evac_g = 0
    t_ydma = 0
    for b in range(4):
        for kg in range(16):
            p = kg % 2
            for kl in range(4):
                tl = dma("sy", pww[:, kl * 512:(kl + 1) * 512], w2[b, :, kg * 4 + kl, :],
                         waits=[("P", t_pww_rd), ("V", t_evac_v), ("P", t_evac_g)]
                         if kl == 0 else (), chain=(kl == 0))
            for hf in range(2):
                h0 = hf * 1024
                t_q1 = op("ve", lambda e, h0=h0: e.tensor_scalar(
                    ftmp[0][:, :], pww[:, h0:h0 + 1024], c1col[:, 1:2], None, ALU.mult),
                    waits=[("L", tl), ("S", t_xi_rd)])
                t_q2 = op("gq", lambda e: e.tensor_copy(xi32[:, :], ftmp[0][:, :]),
                          waits=[("V", t_q1)])
                t_xi_rd = op("se", lambda e, p=p, h0=h0: e.copy(
                    pwb[p][:, h0:h0 + 1024], xi32[:, :]),
                    waits=[("P", t_q2), ("T", t_pwb_rd.get(p, 0)) if h0 == 0 else ("P", t_q2)])
            last_q[p] = t_q2
            t_pww_rd = t_q2
            t_wq = t_xi_rd
            for kl in range(4):
                ki = kg * 4 + kl
                if ki < X2_RES:
                    rhs_t = X2S[ki]
                    wrhs = [("S", t_x2q)]
                else:
                    sp = ki % 2
                    trl = dma("sy", STG[sp][:, :], d_x2[ki - X2_RES],
                              waits=[("S", last_stg.get(sp, 0)), ("T", t_stg_rd.get(sp, 0))])
                    t_m = op("ve", lambda e, sp=sp: e.tensor_tensor(
                        ftmp[1][:, :], STG[sp][:, :], r_bc[:, :], ALU.mult),
                        waits=[("L", trl), ("G", t_ydma), ("S", t_xi_rd)])
                    t_m = op("ve", lambda e: e.tensor_scalar(ftmp[1][:, :], ftmp[1][:, :], Q127,
                                                             -Q127, ALU.min, ALU.max))
                    t_i = op("gq", lambda e: e.tensor_copy(xi32[:, :], ftmp[1][:, :]),
                             waits=[("V", t_m), ("S", t_xi_rd)])
                    t_xi_rd = op("se", lambda e, sp=sp: e.copy(STG[sp][:, :], xi32[:, :]),
                                 waits=[("P", t_i)])
                    last_stg[sp] = t_xi_rd
                    rhs_t = STG[sp]
                    wrhs = [("S", last_stg[sp])]
                for r in range(4):
                    for cg in range(NCG):
                        h0 = cg * 512
                        op("te", lambda e, r=r, cg=cg, kl=kl, p=p, rhs_t=rhs_t, h0=h0, ki=ki:
                           e.matmul(pp[2 * r + cg][:, :],
                                    pwb[p][:, kl * 512 + 128 * r:kl * 512 + 128 * r + 128],
                                    rhs_t[:, h0:h0 + 512],
                                    start=(ki == 0), stop=(ki == HT - 1), skip_group_check=True),
                           waits=(wrhs + [("S", t_wq)]) if (r == 0 and cg == 0) else ())
                if ki >= X2_RES:
                    t_stg_rd[ki % 2] = C["T"]
            t_pwb_rd[p] = C["T"]
        t_mmend = C["T"]
        t_4prev = 0
        for r in range(4):
            mi = 4 * b + r
            tlx = dma("sy", xf32[:, :], xT[128 * mi:128 * (mi + 1), 0:PC],
                      waits=[("V", t_evac_v), ("P", t_evac_g), ("P", t_4prev)])
            for cg in range(NCG):
                h0 = cg * 512
                t_1 = op("ve", lambda e, r=r, cg=cg, h0=h0: e.tensor_tensor(
                    ftmp[0][:, h0:h0 + 512], pp[2 * r + cg][:, :], smb[:, h0:h0 + 512], ALU.mult),
                    waits=[("T", t_mmend), ("G", t_s2bc), ("G", t_ydma)])
                t_2 = op("gq", lambda e, mi=mi, h0=h0: e.tensor_scalar(
                    ftmp[0][:, h0:h0 + 512], ftmp[0][:, h0:h0 + 512], b2s[:, mi:mi + 1], None,
                    ALU.add), waits=[("V", t_1)])
                t_3 = op("ve", lambda e, mi=mi, h0=h0: e.tensor_tensor(
                    ftmp[0][:, h0:h0 + 512], ftmp[0][:, h0:h0 + 512], AOR[mi][:, h0:h0 + 512],
                    ALU.add), waits=[("P", t_2), ("L", t_aor_all)])
                t_4 = op("gq", lambda e, mi=mi, h0=h0: e.tensor_tensor(
                    ftmp[1][:, h0:h0 + 512], ftmp[0][:, h0:h0 + 512], xf32[:, h0:h0 + 512],
                    ALU.add), waits=[("V", t_3), ("L", tlx), ("G", t_ydma)])
                t_ydma = dma("gq", y[128 * mi:128 * (mi + 1), h0:h0 + 512],
                             ftmp[1][:, h0:h0 + 512], waits=[("P", t_4)])
                t_4prev = t_4
        t_evac_v = C["V"]
        t_evac_g = C["P"]
    final_g = C["G"]
    cmds["gq"].append(lambda e, v=final_g: e.wait_ge(sems["G"], v))

    with es:
        block = es.enter_context(nc.Block())

        @block.sync
        def _(e):
            for f in cmds["sy"]:
                f(e)

        @block.vector
        def _(e):
            for f in cmds["ve"]:
                f(e)

        @block.scalar
        def _(e):
            for f in cmds["se"]:
                f(e)

        @block.gpsimd
        def _(e):
            for f in cmds["gq"]:
                f(e)

        @block.tensor
        def _(e):
            for f in cmds["te"]:
                f(e)

    return nc


def kernel(L0_attn_hidden_states, L0_attn_Wq, L0_attn_bq, L0_attn_Wk, L0_attn_bk,
           L0_attn_Wv, L0_attn_bv, L0_attn_Wo, L0_ffn_W1, L0_ffn_B1, L0_ffn_W2, L0_ffn_B2):
    f = np.float32
    x = np.asarray(L0_attn_hidden_states, f)[0]
    xTf = np.ascontiguousarray(x.T)
    wq_h = np.ascontiguousarray(np.asarray(L0_attn_Wq, f).reshape(16, 128, 16, 128)
                                .transpose(0, 3, 2, 1).reshape(16, 128, 2048))
    wo_h = np.ascontiguousarray(np.asarray(L0_attn_Wo, f).reshape(16, 128, 16, 128)
                                .transpose(0, 3, 2, 1).reshape(16, 128, 2048))
    w1_h = np.ascontiguousarray(np.asarray(L0_ffn_W1, f).reshape(64, 128, 16, 128)
                                .transpose(0, 3, 2, 1).reshape(64, 128, 2048))
    w2_h = np.ascontiguousarray(np.asarray(L0_ffn_W2, f).reshape(4, 4, 128, 64, 128)
                                .transpose(0, 4, 3, 1, 2).reshape(4, 128, 64, 512))
    wk_h = np.ascontiguousarray(np.asarray(L0_attn_Wk, f).T.reshape(16, 128, 512))
    wv_h = np.ascontiguousarray(np.asarray(L0_attn_Wv, f).T.reshape(16, 128, 512))
    bq_h = np.ascontiguousarray(np.asarray(L0_attn_bq, f).reshape(16, 128).T)
    bkv_h = np.ascontiguousarray(
        np.concatenate([np.asarray(L0_attn_bk, f), np.asarray(L0_attn_bv, f)])[None, :])
    b1_h = np.ascontiguousarray(np.asarray(L0_ffn_B1, f).reshape(64, 128).T)
    b2_h = np.ascontiguousarray(np.asarray(L0_ffn_B2, f).reshape(16, 128).T)
    idn = np.eye(128, dtype=f)

    nc = build_nc()
    in_maps = []
    for c in range(NCORE):
        xTc = np.ascontiguousarray(np.roll(xTf, -PC * c, axis=1))
        in_maps.append(dict(xT=xTc, wq=wq_h, wo=wo_h, w1=w1_h, w2=w2_h, wk=wk_h, wv=wv_h,
                            bqh=bq_h, bkv=bkv_h, b1h=b1_h, b2h=b2_h, idn=idn))
    res = run_bass_kernel_spmd(nc, in_maps, core_ids=list(range(NCORE)))
    out = np.empty((S, D), f)
    for c in range(NCORE):
        out[PC * c:PC * (c + 1), :] = res.results[c]["y"].T
    return out.reshape(1, S, D)
